# version 34
# baseline (speedup 1.0000x reference)
import numpy as np
import concourse.bass as bass
import concourse.mybir as mybir
from concourse.bass_utils import run_bass_kernel_spmd

F32 = mybir.dt.float32
BF16 = mybir.dt.bfloat16
I32 = mybir.dt.int32
AF = mybir.ActivationFunctionType
ALU = mybir.AluOpType

P = 128
D = 1024
DC = D // P
H3 = 3072
H3C = H3 // P
HEADS = 16
DH = 64
L = 2048
SEQS = 2
SCALE_K = 0.125
SQ = 1.0 / 127.0
EPS = 1e-5

ENGINES = ("pe", "dve", "act", "pool", "sp")
DMA_SEMS = ("d_st", "d_x0", "d_x1", "d_x2", "d_x3",
            "d_o0", "d_o1",
            "d_wg0", "d_wg1", "d_wu0", "d_wu1", "d_wd0", "d_wd1", "d_wd2", "d_wd3",
            "d_cg0", "d_cg1", "d_cu0", "d_cu1", "d_cd0", "d_cd1")


class Ev:
    __slots__ = ("sem", "eng", "val", "needed")

    def __init__(self, sem, eng):
        self.sem, self.eng, self.val, self.needed = sem, eng, None, False


class Res:
    __slots__ = ("w", "rs")

    def __init__(self):
        self.w, self.rs = None, []


class Plan:
    def __init__(self):
        self.q = {e: [] for e in ENGINES}
        self.last = {e: None for e in ENGINES}
        self.dma_cnt = {s: 0 for s in DMA_SEMS}

    def op(self, eng, emit, reads=(), writes=(), scalar_reads=(), dma=None,
           extra_waits=(), writes_nowait=()):
        waits = []

        def need(ev, force=False):
            if ev is None:
                return
            if ev.eng == eng and not force and eng == "pe":
                return
            ev.needed = True
            waits.append(ev)

        for r in reads:
            need(r.w)
        for r in scalar_reads:
            need(r.w, force=True)
        for w in writes:
            need(w.w)
            for ev in w.rs:
                need(ev)
        for ev in extra_waits:
            need(ev, force=(ev.eng == eng))
        if dma:
            ev = Ev(dma, None)
            self.dma_cnt[dma] += 16
            ev.val = self.dma_cnt[dma]
            ev.needed = True
        else:
            ev = Ev(eng, eng)
        self.q[eng].append((waits, emit, ev, bool(dma)))
        self.last[eng] = ev
        for r in list(reads) + list(scalar_reads):
            r.rs.append(ev)
        for w in list(writes) + list(writes_nowait):
            w.w = ev
            w.rs = []
        return ev

    def assign(self):
        for eng in ENGINES:
            cnt = 0
            for waits, emit, ev, dma in self.q[eng]:
                if not dma and ev.needed:
                    cnt += 1
                    ev.val = cnt

    def emit(self, eng_obj, eng, sems):
        satisfied = {}
        for waits, emit, ev, dma in self.q[eng]:
            best = {}
            for w in waits:
                if w.val is None:
                    continue
                v = self.dma_cnt["d_st"] if w.sem == "d_st" else w.val
                if v > best.get(w.sem, 0):
                    best[w.sem] = v
            for s, v in best.items():
                if satisfied.get(s, 0) < v:
                    eng_obj.wait_ge(sems[s], v)
                    satisfied[s] = v
            instr = emit(eng_obj)
            if ev.needed and instr is not None:
                instr.then_inc(sems[ev.sem], 16 if dma else 1)


def build_core_program(n_halves=4, do_b=True, a_ph=9, a_ch=4):
    nc = bass.Bass("TRN2", target_bir_lowering=False, debug=False)

    x_h = nc.dram_tensor("x", [SEQS, L, D], F32, kind="ExternalInput").ap()
    wqkvT_h = nc.dram_tensor("wqkvT", [D, H3], F32, kind="ExternalInput").ap()
    qkvb_h = nc.dram_tensor("qkvb", [1, H3], F32, kind="ExternalInput").ap()
    woT_h = nc.dram_tensor("woT", [D, D], F32, kind="ExternalInput").ap()
    ob_h = nc.dram_tensor("ob", [1, D], F32, kind="ExternalInput").ap()
    wgT_h = nc.dram_tensor("wgT", [D, H3], I32, kind="ExternalInput").ap()
    wuT_h = nc.dram_tensor("wuT", [D, H3], I32, kind="ExternalInput").ap()
    wdT_h = nc.dram_tensor("wdT", [H3, D], I32, kind="ExternalInput").ap()
    out_h = nc.dram_tensor("out", [SEQS, L, D], F32, kind="ExternalOutput").ap()
    wg_scr = nc.dram_tensor("wg_scr", [H3C, P, DC * P], BF16, kind="Internal").ap()
    wu_scr = nc.dram_tensor("wu_scr", [H3C, P, DC * P], BF16, kind="Internal").ap()
    wd_scr = nc.dram_tensor("wd_scr", [H3C, P, D], BF16, kind="Internal").ap()

    _probe = nc.alloc_sbuf_tensor("_base_probe", [P, 4], F32)
    cur = [nc.lookup_mloc(_probe).addr]

    def at(name, shape, dtype, align=64):
        o = (cur[0] + align - 1) // align * align
        t = nc.alloc_sbuf_tensor_at(name, shape, dtype, offset=o)
        n = 1
        for s in shape[1:]:
            n *= s
        cur[0] = o + n * mybir.dt.size(dtype)
        return t

    wqkv_bf = at("wqkv_bf", [P, DC, H3], BF16)
    wo_bf = at("wo_bf", [P, DC, D], BF16)
    x1_tm = at("x1_tm", [P, 8, D], BF16)
    ident_b = at("ident_b", [P, P], BF16)
    ident_f = at("ident_f", [P, P], F32)
    mask4 = at("mask4", [P, 4, P], BF16)
    qkb_cols = at("qkb_cols", [P, 16], F32)
    ones_bf = at("ones_bf", [1, P], BF16)
    zeros_bf = at("zeros_bf", [1, 512], BF16)
    vb_bcast = at("vb_bcast", [P, D], BF16)
    ob_bf = at("ob_bf", [1, D], BF16)
    S_sb = at("S_sb", [P, 512], BF16)
    wgr = at("wgr", [P, 2, DC, P], BF16)
    wur = at("wur", [P, 2, DC, P], BF16)
    wdr = at("wdr", [P, 4, D], BF16)
    out_sb = at("out_sb", [P, 2, D], F32)
    silu_s = at("silu_s", [P, 2, 512], BF16)
    h1f = at("h1f", [P, 3, D], F32)
    sqs = nc.alloc_sbuf_tensor_at("sqs", [P, D], BF16,
                                  offset=nc.lookup_mloc(silu_s).addr)
    stats = at("stats", [P, 3, 8], F32)
    x1_fm = at("x1_fm", [P, DC, 1024], BF16)
    persist_end = cur[0]

    scratch0 = cur[0]
    x_fm = at("x_fm", [P, DC, 512], BF16)
    qk_fm = at("qk_fm", [P, 16, 512], BF16)
    at_sb = at("at_sb", [P, HEADS, P], BF16)
    y_fm = at("y_fm", [P, DC, P], BF16)
    v_tm = at("v_tm", [P, 4, D], BF16)
    k_tm = at("k_tm", [P, 4, D], BF16)
    qkvb_row = nc.alloc_sbuf_tensor_at("qkvb_row", [1, H3], F32,
                                       offset=scratch0 + 8192)
    vb_row = nc.alloc_sbuf_tensor_at("vb_row", [1, D], BF16,
                                     offset=scratch0 + 8192 + 12288)
    cur[0] = scratch0 + 48 * 1024
    x_tm = at("x_tm", [P, 4, D], F32)
    sweepA_end = cur[0]

    cur[0] = scratch0
    h_fm = at("h_fm", [P, H3C, 1024], BF16)
    sweepB_end = cur[0]

    assert max(sweepA_end, sweepB_end) <= 229376, (persist_end, sweepA_end, sweepB_end)

    pF = nc.alloc_psum_tensor("pF", [P, 7, 512], F32)
    pB = nc.alloc_psum_tensor("pB", [P, 1024], BF16)

    pl = Plan()
    h1rr = [0]

    def h1_next():
        s = h1rr[0] % 3
        h1rr[0] += 1
        return s
    r_wqkv, r_wo = Res(), Res()
    r_consts = Res()
    r_qkbcols = Res()
    r_qrow = Res()
    r_qbf = Res()
    r_obf = Res()
    r_Ssb = Res()
    r_x_tm = Res()
    r_x_tm_s = [Res() for _ in range(4)]
    r_x_fm = Res()
    r_qk = [Res() for _ in range(16)]
    r_v = [Res() for _ in range(4)]
    r_k = [Res() for _ in range(4)]
    r_at = Res()
    r_at_e, r_at_o = Res(), Res()
    r_y = Res()
    r_h1 = [Res(), Res(), Res()]
    r_sqs = Res()
    r_stats = [Res(), Res(), Res()]
    r_x1tm = [Res() for _ in range(8)]
    r_x1fm = [Res(), Res()]
    r_hfm = [Res() for _ in range(H3C)]
    rh_xfm = r_hfm[0:4]
    rh_qk = r_hfm[4:12]
    rh_at = r_hfm[12:14]
    rh_y = r_hfm[14:15]
    rh_v = r_hfm[15:19]
    rh_k = r_hfm[19:23]
    r_silu = [Res(), Res()]
    r_outsb = [Res(), Res()]
    r_wgr = [Res(), Res()]
    r_wur = [Res(), Res()]
    r_wdr = [Res() for _ in range(4)]
    rb = [Res() for _ in range(7)]
    r_pBh = [Res(), Res()]

    overlay = [r_x_tm, *r_x_tm_s, r_x_fm, r_qk, *r_v, *r_k,
               r_at, r_y, *r_hfm]

    def barrier():
        ev = pl.op("pe", lambda e: e.nop(),
                   extra_waits=[x for x in (pl.last["dve"], pl.last["act"],
                                            pl.last["pool"],
                                            last_store[0], last_store[1])
                                if x is not None])
        for r in overlay:
            r.w = ev
            r.rs = []

    def dma(eng, out, in_, **kw):
        return lambda e: e.dma_start(out=out, in_=in_, **kw)

    pl.op("pool", dma("pool", qkvb_row[:], qkvb_h), writes_nowait=[r_qrow], dma="d_st")
    pl.op("pool", dma("pool", ob_bf[:], ob_h), writes_nowait=[r_obf], dma="d_st")

    pl.op("pool", lambda e: e.memset(ident_b[:], 0.0), writes=[r_consts])
    pl.op("pool", lambda e: e.affine_select(
        out=ident_b[:], in_=ident_b[:], compare_op=ALU.not_equal,
        fill=1.0, base=0, pattern=[[-1, P]], channel_multiplier=1),
        writes=[r_consts])
    pl.op("pool", lambda e: e.memset(ident_f[:], 0.0), writes=[r_consts])
    pl.op("pool", lambda e: e.affine_select(
        out=ident_f[:], in_=ident_f[:], compare_op=ALU.not_equal,
        fill=1.0, base=0, pattern=[[-1, P]], channel_multiplier=1),
        writes=[r_consts])
    pl.op("pool", lambda e: e.memset(mask4[:], 1.0), writes=[r_consts])
    pl.op("pool", lambda e: e.affine_select(
        out=mask4[:], in_=mask4[:], compare_op=ALU.is_ge,
        fill=0.0, base=0, pattern=[[0, 4], [1, P]], channel_multiplier=-1),
        writes=[r_consts])
    pl.op("pool", lambda e: e.memset(ones_bf[:], 1.0), writes=[r_consts])
    pl.op("pool", lambda e: e.memset(zeros_bf[:], 0.0), writes=[r_consts])

    for kc in range(DC):
        pl.op("pool", dma("pool", wqkv_bf[:, kc, :],
                          wqkvT_h[kc * P:(kc + 1) * P, :],
                          max_dma_last_dim=8192),
              writes_nowait=[r_wqkv], dma="d_st")
    for kc in range(DC):
        pl.op("pool", dma("pool", wo_bf[:, kc, :],
                          woT_h[kc * P:(kc + 1) * P, :],
                          max_dma_last_dim=8192),
              writes_nowait=[r_wo], dma="d_st")

    def qkb_tr(e):
        last = None
        for oc in range(16):
            last = e.transpose(pF[:, 1, oc:oc + 1],
                               qkvb_row[0:1, oc * P:(oc + 1) * P],
                               ident_f[0:1, 0:1])
        return last

    pl.op("pe", qkb_tr, reads=[r_qrow, r_consts], writes=[rb[1]])
    pl.op("dve", lambda e: e.tensor_copy(qkb_cols[:], pF[:, 1, 0:16]),
          reads=[rb[1]], writes=[r_qkbcols])
    pl.op("dve", lambda e: e.tensor_scalar_mul(qkb_cols[:, 8:16],
                                               qkb_cols[:, 8:16], SCALE_K),
          writes=[r_qkbcols], scalar_reads=[r_qkbcols])
    pl.op("pool", lambda e: e.tensor_copy(vb_row[:], qkvb_row[0:1, 2048:3072]),
          reads=[r_qrow], writes=[r_qbf])
    def vb_mm(e):
        e.matmul(pF[:, 5, :], ones_bf[0:1, :], vb_row[0:1, 0:512],
                 start=True, stop=True)
        return e.matmul(pF[:, 6, :], ones_bf[0:1, :], vb_row[0:1, 512:1024],
                        start=True, stop=True)
    pl.op("pe", vb_mm, reads=[r_qbf, r_consts], writes=[rb[5], rb[6]])
    pl.op("dve", lambda e: e.tensor_copy(
        vb_bcast[:], pF[:, 5:7, :].rearrange("p a n -> p (a n)")),
        reads=[rb[5], rb[6]], writes=[r_qbf])

    wg_src = wgT_h.rearrange("(kc p) h -> p kc h", p=P)
    wu_src = wuT_h.rearrange("(kc p) h -> p kc h", p=P)
    wd_src = wdT_h.rearrange("(hc p) d -> p hc d", p=P)

    r_scr_g = [Res() for _ in range(H3C)]
    r_scr_u = [Res() for _ in range(H3C)]
    r_scr_d = [Res() for _ in range(H3C)]

    r_cvg = [Res(), Res()]
    r_cvu = [Res(), Res()]
    r_cvd = [Res(), Res()]

    def wconv_thunks():
        thunks = []

        def gu_pair(i):
            s = slice(i * P, (i + 1) * P)
            pl.op("pool", dma("pool",
                              wg_scr[i].rearrange("p (a b) -> p a b", a=DC),
                              wg_src[:, :, s]),
                  writes=[r_cvg[i % 2]], writes_nowait=[r_scr_g[i]],
                  dma="d_cg%d" % (i % 2))
            pl.op("pool", dma("pool",
                              wu_scr[i].rearrange("p (a b) -> p a b", a=DC),
                              wu_src[:, :, s]),
                  writes=[r_cvu[i % 2]], writes_nowait=[r_scr_u[i]],
                  dma="d_cu%d" % (i % 2))

        def wd_piece(hc):
            pl.op("pool", dma("pool", wd_scr[hc],
                              wdT_h[hc * P:(hc + 1) * P, :]),
                  writes=[r_cvd[hc % 2]], writes_nowait=[r_scr_d[hc]],
                  dma="d_cd%d" % (hc % 2))

        for i in range(H3C):
            thunks.append(lambda i=i: gu_pair(i))
        for hc in range(H3C):
            thunks.append(lambda hc=hc: wd_piece(hc))
        return thunks

    def load_gu(i, slot):
        pl.op("sp", dma("sp", wgr[:, slot, :, :],
                        wg_scr[i].rearrange("p (a b) -> p a b", a=DC)),
              reads=[r_scr_g[i]], writes=[r_wgr[slot]], dma="d_wg%d" % slot)
        pl.op("sp", dma("sp", wur[:, slot, :, :],
                        wu_scr[i].rearrange("p (a b) -> p a b", a=DC)),
              reads=[r_scr_u[i]], writes=[r_wur[slot]], dma="d_wu%d" % slot)

    def load_wd2(hc0, sp):
        pl.op("sp", dma("sp", wdr[:, sp * 2:sp * 2 + 2, :],
                        wd_scr[hc0:hc0 + 2].rearrange("a p d -> p a d")),
              reads=[r_scr_d[hc0], r_scr_d[hc0 + 1]],
              writes=[r_wdr[sp * 2], r_wdr[sp * 2 + 1]],
              dma="d_wd%d" % sp)

    def a_head_ops(seq, half, t, with_loader=False):
        tok0 = half * 1024 + t * 512
        gt = (seq * 2 + half) * 2 + t
        thunks = []

        def load_piece(s):
            sl = (gt + s) % 4
            pl.op("sp", dma("sp", x_tm[:, sl, :],
                            x_h[seq, tok0 + s * P:tok0 + (s + 1) * P, :]),
                  writes=[r_x_tm_s[sl]], dma="d_x%d" % sl)

        def loads():
            for s in range(4):
                load_piece(s)
        thunks.append(('x', loads))
        thunks.append(('piece_loader', load_piece))

        def xtrs(s0):
            for s in (s0, s0 + 1):
                sl = (gt + s) % 4

                def xtr(e, sl=sl):
                    last = None
                    for fc in range(DC):
                        last = e.transpose(
                            pF[:, 1 + fc // 4, (fc % 4) * P:(fc % 4 + 1) * P],
                            x_tm[:, sl, fc * P:(fc + 1) * P], ident_f[:])
                    return last

                pl.op("pe", xtr, reads=[r_x_tm_s[sl], r_consts],
                      writes=[rb[1], rb[2]])

                def xtr_ev(e, s=s):
                    e.tensor_copy(x_fm[:, 0:4, s * P:(s + 1) * P], pF[:, 1, :])
                    return e.tensor_copy(x_fm[:, 4:8, s * P:(s + 1) * P],
                                         pF[:, 2, :])

                pl.op("dve", xtr_ev, reads=[rb[1], rb[2]],
                      writes=[r_x_fm] + rh_xfm)
        thunks.append(('q', lambda: xtrs(0)))
        thunks.append(('q', lambda: xtrs(2)))

        def qkvs(oc0):
            for oc in range(oc0, oc0 + 4):
                b = 1 + (oc % 2)

                def qk_mm(e, oc=oc, b=b):
                    last = None
                    for kc in range(DC):
                        last = e.matmul(pF[:, b, :],
                                        wqkv_bf[:, kc, oc * P:(oc + 1) * P],
                                        x_fm[:, kc, :],
                                        start=(kc == 0), stop=(kc == DC - 1))
                    return last

                pl.op("pe", qk_mm, reads=[r_x_fm, r_wqkv], writes=[rb[b]])
                if oc >= 8:
                    def qk_ev(e, oc=oc, b=b):
                        return e.activation(qk_fm[:, oc, :], pF[:, b, :],
                                            AF.Identity,
                                            bias=qkb_cols[:, oc:oc + 1],
                                            scale=SCALE_K)

                    pl.op("act", qk_ev, reads=[rb[b]],
                          writes=[r_qk[oc]] + rh_qk,
                          scalar_reads=[r_qkbcols])
                else:
                    def qk_ev(e, oc=oc, b=b):
                        return e.tensor_scalar(qk_fm[:, oc, :], pF[:, b, :],
                                               1.0, qkb_cols[:, oc:oc + 1],
                                               ALU.mult, ALU.add)

                    pl.op("dve", qk_ev, reads=[rb[b]],
                          writes=[r_qk[oc]] + rh_qk,
                          scalar_reads=[r_qkbcols])
        for oc0 in (8, 12, 0, 4):
            thunks.append(('q', lambda oc0=oc0: qkvs(oc0)))
        loader = thunks.pop(1)[1]
        if with_loader:
            return thunks, loader
        return thunks

    def emit_x1tr(gc):
        def x1tr(e, gc=gc):
            last = None
            for fc in range(DC):
                last = e.transpose(pB[:, fc * P:(fc + 1) * P],
                                   x1_tm[:, gc, fc * P:(fc + 1) * P],
                                   ident_b[:])
            return last

        pl.op("pe", x1tr, reads=[r_x1tm[gc], r_consts], writes=r_pBh)
        pl.op("dve", lambda e, gc=gc: e.tensor_copy(
            x1_fm[:, :, gc * P:(gc + 1) * P],
            pB[:].rearrange("p (a b) -> p a b", a=DC)),
            reads=r_pBh, writes=[r_x1fm[gc // 4]])

    def a_tile_rest(seq, half, t, x1tr_pending, early=()):
        early = list(early)
        """V+ktr phase and the software-pipelined chunk loop for tile t."""
        gt = (seq * 2 + half) * 2 + t
        for s in range(4):
            def v_mm(e, s=s):
                last = None
                for n2 in range(2):
                    b = 3 + n2
                    cols = slice(2048 + n2 * 512, 2048 + (n2 + 1) * 512)
                    for kc in range(DC):
                        last = e.matmul(pF[:, b, :],
                                        x_fm[:, kc, s * P:(s + 1) * P],
                                        wqkv_bf[:, kc, cols],
                                        start=(kc == 0), stop=(kc == DC - 1))
                return last

            pl.op("pe", v_mm, reads=[r_x_fm, r_wqkv, r_qbf, r_consts],
                  writes=[rb[3], rb[4]])
            pl.op("act", lambda e, s=s: e.activation(
                v_tm[:, s, :],
                pF[:, 3:5, :].rearrange("p a n -> p (a n)"), AF.Copy),
                reads=[rb[3], rb[4]], writes=[r_v[s]] + rh_v)
            pl.op("pool", lambda e, s=s: e.tensor_add(
                v_tm[:, s, :], v_tm[:, s, :], vb_bcast[:]),
                reads=[r_qbf], writes=[r_v[s]])

            def ktr(e, c=s):
                last = None
                for fc in range(DC):
                    last = e.transpose(pB[:, fc * P:(fc + 1) * P],
                                       qk_fm[:, 8 + fc, c * P:(c + 1) * P],
                                       ident_b[:])
                return last

            pl.op("pe", ktr, reads=r_qk[8:16] + [r_consts], writes=r_pBh)
            pl.op("act", lambda e, c=s: e.activation(k_tm[:, c, :], pB[:],
                                                     AF.Copy),
                  reads=r_pBh, writes=[r_k[s]] + rh_k)

        def at_slot(h):
            return (h % 2) * 8 + (h // 8) * 4 + (h % 8) // 2

        def at_par_f(c, par):
            def at_par(e):
                last = None
                if par == 1:
                    e.drain()
                for h in range(par, HEADS, 2):
                    sl = at_slot(h)
                    bank = 1 + sl // 4
                    q = sl % 4
                    qc = h // 2
                    r0 = 64 * par
                    last = e.matmul(
                        pF[:, bank, q * P:(q + 1) * P],
                        qk_fm[r0:r0 + 64, 8 + qc, c * P:(c + 1) * P],
                        qk_fm[r0:r0 + 64, qc, c * P:(c + 1) * P],
                        start=(q == 0), stop=(q == 3),
                        skip_group_check=True)
                return last
            return at_par

        def y_zero_f():
            def y_zero(e):
                e.matmul(pF[:, 5, :], ones_bf[0:1, :], zeros_bf[0:1, :],
                         start=True, stop=False, skip_group_check=True)
                return e.matmul(pF[:, 6, :], ones_bf[0:1, :], zeros_bf[0:1, :],
                                start=True, stop=False, skip_group_check=True)
            return y_zero

        def y_intra_f(c, g, par):
            def y_intra(e):
                last = None
                for h in range(par, HEADS, 2):
                    pair = h // 2
                    r0 = 64 * (h % 2)
                    last = e.matmul(
                        pF[r0:r0 + 64, 5 + pair // 4,
                           (pair % 4) * P:(pair % 4 + 1) * P],
                        v_tm[:, c, h * 64:(h + 1) * 64],
                        at_sb[:, at_slot(h), :],
                        start=False, stop=(g == 0),
                        skip_group_check=True)
                return last
            return y_intra

        def y_cross_f(c):
            def y_cross(e):
                last = None
                for h in range(HEADS):
                    pair = h // 2
                    r0 = 64 * (h % 2)
                    last = e.matmul(
                        pF[r0:r0 + 64, 5 + pair // 4,
                           (pair % 4) * P:(pair % 4 + 1) * P],
                        S_sb[r0:r0 + 64, pair * 64:(pair + 1) * 64],
                        qk_fm[r0:r0 + 64, pair, c * P:(c + 1) * P],
                        start=False, stop=True,
                        skip_group_check=True)
                return last
            return y_cross

        def s_upd_f(c, g):
            def s_upd(e):
                last = None
                for h in range(HEADS):
                    r0 = 64 * (h % 2)
                    pair = h // 2
                    last = e.matmul(
                        pF[r0:r0 + 64, 0, pair * 64:(pair + 1) * 64],
                        k_tm[:, c, h * 64:(h + 1) * 64],
                        v_tm[:, c, h * 64:(h + 1) * 64],
                        start=False, stop=(g == 14),
                        skip_group_check=True)
                return last
            return s_upd

        def op_bias_f():
            def op_bias(e):
                e.matmul(pF[:, 5, :], ones_bf[0:1, :], ob_bf[0:1, 0:512],
                         start=True, stop=False, skip_group_check=True)
                return e.matmul(pF[:, 6, :], ones_bf[0:1, :],
                                ob_bf[0:1, 512:1024],
                                start=True, stop=False, skip_group_check=True)
            return op_bias

        def op_mm_f():
            def op_mm(e):
                last = None
                for n2 in range(2):
                    b = 5 + n2
                    cols = slice(n2 * 512, (n2 + 1) * 512)
                    for kc in range(DC):
                        last = e.matmul(pF[:, b, :], y_fm[:, kc, :],
                                        wo_bf[:, kc, cols],
                                        start=False, stop=(kc == DC - 1),
                                        skip_group_check=True)
                return last
            return op_mm

        for it in range(5):
            c = it
            g = (half * 2 + t) * 4 + c
            if c <= 3:
                if g > 0:
                    pl.op("act", lambda e: e.activation(
                        S_sb[:], pF[:, 0, :], AF.Copy),
                        reads=[rb[0]], writes=[r_Ssb])
                pl.op("pe", at_par_f(c, 0), reads=list(r_qk),
                      writes=[rb[1], rb[2]])
                pl.op("dve", lambda e: e.tensor_tensor(
                    at_sb[:, 0:4, :], pF[:, 1, :], mask4[:], ALU.mult),
                    reads=[rb[1], r_consts], writes=[r_at_e] + rh_at)
                pl.op("act", lambda e: e.activation(
                    at_sb[:, 4:8, :], pF[:, 2, :], AF.Copy),
                    reads=[rb[2]], writes=[r_at_e] + rh_at)
                pl.op("pool", lambda e: e.tensor_tensor(
                    at_sb[:, 4:8, :], at_sb[:, 4:8, :], mask4[:], ALU.mult),
                    reads=[r_consts], writes=[r_at_e] + rh_at)
                pl.op("pe", at_par_f(c, 1), reads=list(r_qk),
                      writes=[rb[3], rb[4]])
                pl.op("dve", lambda e: e.tensor_tensor(
                    at_sb[:, 8:12, :], pF[:, 3, :], mask4[:], ALU.mult),
                    reads=[rb[3], r_consts], writes=[r_at_o] + rh_at)
                pl.op("act", lambda e: e.activation(
                    at_sb[:, 12:16, :], pF[:, 4, :], AF.Copy),
                    reads=[rb[4]], writes=[r_at_o] + rh_at)
                pl.op("pool", lambda e: e.tensor_tensor(
                    at_sb[:, 12:16, :], at_sb[:, 12:16, :], mask4[:],
                    ALU.mult),
                    reads=[r_consts], writes=[r_at_o] + rh_at)
            if it >= 1:
                cp = it - 1
                pl.op("pe", op_bias_f(), reads=[r_obf, r_consts],
                      writes=[rb[5], rb[6]])
                pl.op("pe", op_mm_f(), reads=[r_y, r_wo],
                      writes=[rb[5], rb[6]])
                slp = (gt + cp) % 4
                par = h1_next()
                pl.op("dve", lambda e, slp=slp, par=par: e.scalar_tensor_tensor(
                    h1f[:, par, :],
                    pF[:, 5:7, :].rearrange("p a n -> p (a n)"),
                    1.0, x_tm[:, slp, :], ALU.mult, ALU.add,
                    accum_out=stats[:, par, 0:1]),
                    reads=[rb[5], rb[6], r_x_tm_s[slp]],
                    writes=[r_h1[par], r_stats[par]])
                lim = 0 if it == 4 else 2
                while len(x1tr_pending) > lim:
                    emit_x1tr(x1tr_pending.pop(0))
                ln_norm(x1_tm[:, t * 4 + cp, :], [r_x1tm[t * 4 + cp]],
                        par)
                for _ in range(2):
                    if wconv:
                        wconv.pop(0)()
                x1tr_pending.append(t * 4 + cp)
                if early and 2 <= it <= 4:
                    early[0](it - 2)
            if c <= 3:
                pl.op("pe", y_zero_f(), reads=[r_consts],
                      writes=[rb[5], rb[6]])
                if g < 15:
                    pl.op("pe", s_upd_f(c, g), reads=[r_k[c], r_v[c]],
                          writes=[rb[0]])
                pl.op("pe", y_intra_f(c, g, 0),
                      reads=[r_v[c], r_at_e, r_consts],
                      writes=[rb[5], rb[6]])
                pl.op("pe", y_intra_f(c, g, 1),
                      reads=[r_v[c], r_at_o, r_consts],
                      writes=[rb[5], rb[6]])
                if g > 0:
                    pl.op("pe", y_cross_f(c),
                          reads=[r_Ssb, r_consts] + r_qk[0:8],
                          writes=[rb[5], rb[6]])

                def y_ev(e):
                    e.activation(y_fm[:, 0:4, :], pF[:, 5, :], AF.Copy)
                    return e.activation(y_fm[:, 4:8, :], pF[:, 6, :],
                                        AF.Copy)

                pl.op("act", y_ev, reads=[rb[5], rb[6]],
                      writes=[r_y] + rh_y)
        if early:
            early[0](3)
        if t == 1:
            while len(x1tr_pending) > 1:
                emit_x1tr(x1tr_pending.pop(0))

    def ln_norm(dest_ap, dest_res_list, par):
        src_ap = h1f[:, par, :]
        sres = r_h1[par]
        rst = r_stats[par]
        st = stats[:, par, :]
        pl.op("act", lambda e: e.activation(sqs[:], src_ap, AF.Square,
                                            accum_out=st[:, 1:2]),
              reads=[sres], writes=[r_sqs, rst, r_silu[0], r_silu[1]])
        pl.op("dve", lambda e: e.tensor_scalar_mul(
            st[:, 2:3], st[:, 0:1], -1.0 / D),
            reads=[rst], writes=[rst])
        pl.op("dve", lambda e: e.tensor_scalar(
            st[:, 3:4], st[:, 2:3], st[:, 2:3], -EPS,
            ALU.mult, ALU.add),
            reads=[rst], writes=[rst], scalar_reads=[rst])
        pl.op("dve", lambda e: e.scalar_tensor_tensor(
            st[:, 4:5], st[:, 1:2], 1.0 / D,
            st[:, 3:4], ALU.mult, ALU.subtract),
            reads=[rst], writes=[rst])
        pl.op("act", lambda e: e.activation(st[:, 5:6], st[:, 4:5],
                                            AF.Sqrt),
              reads=[rst], writes=[rst])
        pl.op("dve", lambda e: e.reciprocal(st[:, 6:7], st[:, 5:6]),
              reads=[rst], writes=[rst])
        pl.op("pool", lambda e: e.tensor_scalar(dest_ap, src_ap,
                                                st[:, 2:3], st[:, 6:7],
                                                ALU.add, ALU.mult),
              reads=[sres], writes=dest_res_list, scalar_reads=[rst])

    def sweep_b(seq, half, a_tail, next_head, first=False, x1p=None):
        if first:
            load_gu(0, 0)
            load_gu(1, 1)
        heads = list(next_head) if next_head else []
        if heads:
            heads.pop(0)[1]()
        if not first:
            load_wd2(0, 0)
        for i in range(H3C):
            slot = i % 2
            for t2 in range(2):
                bg, bu = 1 + 2 * t2, 2 + 2 * t2

                def g_mm(e, i=i, t2=t2, slot=slot, b=bg):
                    last = None
                    for kc in range(DC):
                        last = e.matmul(pF[:, b, :], wgr[:, slot, kc, :],
                                        x1_fm[:, kc, t2 * 512:(t2 + 1) * 512],
                                        start=(kc == 0), stop=(kc == DC - 1))
                    return last

                def u_mm(e, i=i, t2=t2, slot=slot, b=bu):
                    last = None
                    for kc in range(DC):
                        last = e.matmul(pF[:, b, :], wur[:, slot, kc, :],
                                        x1_fm[:, kc, t2 * 512:(t2 + 1) * 512],
                                        start=(kc == 0), stop=(kc == DC - 1))
                    return last

                pl.op("pe", g_mm, reads=[r_x1fm[t2], r_wgr[slot]],
                      writes=[rb[bg]])
                pl.op("pe", u_mm, reads=[r_x1fm[t2], r_wur[slot]],
                      writes=[rb[bu]])
                if x1p and i == 0 and t2 == 0:
                    emit_x1tr(x1p.pop(0))
                pl.op("act", lambda e, t2=t2, b=bg: e.activation(
                    silu_s[:, t2, :], pF[:, b, :], AF.Silu, scale=SQ),
                    reads=[rb[bg]], writes=[r_silu[t2]])
                pl.op("dve", lambda e, i=i, t2=t2, b=bu: e.tensor_tensor(
                    h_fm[:, i, t2 * 512:(t2 + 1) * 512], silu_s[:, t2, :],
                    pF[:, b, :], ALU.mult),
                    reads=[r_silu[t2], rb[bu]], writes=[r_hfm[i]],
                    extra_waits=(a_tail if (i == 0 and t2 == 0) else ()))
            if i + 2 < H3C:
                load_gu(i + 2, slot)
            if not first:
                if i == 0:
                    load_wd2(2, 1)
            elif i in (8, 10):
                load_wd2(i - 8, (i - 8) // 2)
            for _ in range(2):
                if wconv:
                    wconv.pop(0)()

        groups = [(0, 1, 2), (3, 4, 5), (6, 7)]
        hc_orders = [list(range(H3C)), list(range(H3C - 1, -1, -1)),
                     list(range(H3C))]
        pairs = []
        for g2 in range(3):
            rng = range(0, H3C, 2) if g2 != 1 else range(H3C - 2, -1, -2)
            pairs.extend((g2, p) for p in rng)
        widx = [2]

        def wd_prefetch():
            if widx[0] < len(pairs):
                g2, p0 = pairs[widx[0]]
                widx[0] += 1
                resident = (g2 == 1 and p0 >= H3C - 4) or \
                           (g2 == 2 and p0 < 4)
                if not resident:
                    load_wd2(p0, (p0 // 2) % 2)

        fin_q = []

        def finalize(gc, par):
            def th():
                slot = gc % 2
                ln_norm(out_sb[:, slot, :], [r_outsb[slot]], par)
                tok0 = half * 1024 + gc * P
                ev_st = pl.op("sp", dma("sp", out_h[seq, tok0:tok0 + P, :],
                                        out_sb[:, slot, :]),
                              reads=[r_outsb[slot]], dma="d_o%d" % slot)
                last_store[slot] = ev_st
            return th

        for grp, gcs in enumerate(groups):
            for idx, hc in enumerate(hc_orders[grp]):
                slot = hc % 4
                if idx == 0:
                    for j, gc in enumerate(gcs):
                        def dn0(e, j=j, gc=gc, hc=hc, slot=slot):
                            last = None
                            for n2 in range(2):
                                last = e.matmul(
                                    pF[:, 1 + 2 * j + n2, :],
                                    h_fm[:, hc, gc * P:(gc + 1) * P],
                                    wdr[:, slot, n2 * 512:(n2 + 1) * 512],
                                    start=True, stop=False,
                                    skip_group_check=True)
                            return last
                        pl.op("pe", dn0, reads=[r_hfm[hc], r_wdr[slot]],
                              writes=[rb[1 + 2 * j], rb[2 + 2 * j]])
                else:
                    def dn_mm(e, gcs=gcs, hc=hc, slot=slot, idx=idx):
                        last = None
                        for j, gc in enumerate(gcs):
                            for n2 in range(2):
                                last = e.matmul(
                                    pF[:, 1 + 2 * j + n2, :],
                                    h_fm[:, hc, gc * P:(gc + 1) * P],
                                    wdr[:, slot, n2 * 512:(n2 + 1) * 512],
                                    start=False, stop=(idx == H3C - 1),
                                    skip_group_check=True)
                        return last

                    bank_res = [rb[1 + 2 * j + n2] for j in range(len(gcs))
                                for n2 in range(2)]
                    pl.op("pe", dn_mm, reads=[r_hfm[hc], r_wdr[slot]],
                          writes=bank_res)
                if idx % 2 == 1:
                    wd_prefetch()
                if fin_q:
                    fin_q.pop(0)()
            for j, gc in enumerate(gcs):
                par = h1_next()

                def h2_ev(e, j=j, gc=gc, par=par):
                    return e.scalar_tensor_tensor(
                        h1f[:, par, :],
                        pF[:, 1 + 2 * j:3 + 2 * j, :]
                        .rearrange("p a n -> p (a n)"),
                        SQ * SQ, x1_tm[:, gc, :], ALU.mult, ALU.add,
                        accum_out=stats[:, par, 0:1])

                pl.op("dve", h2_ev,
                      reads=[rb[1 + 2 * j], rb[2 + 2 * j], r_x1tm[gc]],
                      writes=[r_h1[par], r_stats[par]])
                fin_q.append(finalize(gc, par))
                while heads and (heads[0][0] == 'x' or grp == 2):
                    heads.pop(0)[1]()
                    break
        while fin_q:
            fin_q.pop(0)()
        for _, th in heads:
            th()

    last_store = [None, None]
    head_done = [False]
    for _, th in a_head_ops(0, 0, 0):
        th()
    head_done[0] = True
    wconv = wconv_thunks()
    n_done = 0
    for seq in range(SEQS):
        if n_done >= n_halves:
            break
        pl.op("pe", lambda e: e.matmul(pF[:, 0, :], ones_bf[0:1, :],
                                       zeros_bf[0:1, :], start=True, stop=True,
                                       skip_group_check=True),
              reads=[r_consts], writes=[rb[0]])
        for half in range(2):
            if n_done >= n_halves:
                break
            if not head_done[0]:
                for _, th in a_head_ops(seq, half, 0):
                    th()
            x1p = []
            heads1, loader1 = a_head_ops(seq, half, 1, with_loader=True)
            a_tile_rest(seq, half, 0, x1p, early=[loader1])
            for tag, th in heads1:
                if tag != 'x':
                    th()
            a_tile_rest(seq, half, 1, x1p)
            head_done[0] = False
            if do_b:
                a_tail = [x for x in (pl.last["pe"],) if x is not None]
                nseq, nhalf = (seq, 1) if half == 0 else (seq + 1, 0)
                nxt = None
                if n_done + 1 < n_halves and nseq < SEQS and do_b:
                    nxt = a_head_ops(nseq, nhalf, 0)
                    head_done[0] = True
                sweep_b(seq, half, a_tail, nxt, first=(n_done == 0),
                        x1p=x1p)
            n_done += 1
            if not (seq == SEQS - 1 and half == 1):
                load_gu(0, 0)
                load_gu(1, 1)

    fw = [e for e in last_store if e is not None]
    for e in fw:
        e.needed = True
    if fw:
        pl.q["sp"].append((fw, lambda e: None, Ev("sp", "sp"), False))

    pl.assign()

    sems = {}
    import contextlib
    ctx = contextlib.ExitStack()
    for name in ENGINES + DMA_SEMS:
        sems[name] = ctx.enter_context(nc.semaphore(name))

    with nc.Block() as blk:
        @blk.tensor
        def _(e):
            pl.emit(e, "pe", sems)

        @blk.vector
        def _(e):
            pl.emit(e, "dve", sems)

        @blk.scalar
        def _(e):
            pl.emit(e, "act", sems)

        @blk.gpsimd
        def _(e):
            pl.emit(e, "pool", sems)

        @blk.sync
        def _(e):
            pl.emit(e, "sp", sems)

    ctx.close()
    return nc


_NC_CACHE = {}


def kernel(L0_attn_x, L0_attn_Wqkv_w, L0_attn_Wqkv_b, L0_attn_out_w,
           L0_attn_out_b, L0_ffn_w_gate, L0_ffn_w_up, L0_ffn_w_down):
    x = np.ascontiguousarray(np.asarray(L0_attn_x, dtype=np.float32))
    wqkvT = np.ascontiguousarray(np.asarray(L0_attn_Wqkv_w, np.float32).T)
    qkvb = np.ascontiguousarray(np.asarray(L0_attn_Wqkv_b, np.float32)
                                .reshape(1, H3))
    woT = np.ascontiguousarray(np.asarray(L0_attn_out_w, np.float32).T)
    ob = np.ascontiguousarray(np.asarray(L0_attn_out_b, np.float32)
                              .reshape(1, D))
    wgT = np.ascontiguousarray(np.asarray(L0_ffn_w_gate, np.int32).T)
    wuT = np.ascontiguousarray(np.asarray(L0_ffn_w_up, np.int32).T)
    wdT = np.ascontiguousarray(np.asarray(L0_ffn_w_down, np.int32).T)

    if "nc" not in _NC_CACHE:
        _NC_CACHE["nc"] = build_core_program()
    nc = _NC_CACHE["nc"]

    in_maps = []
    for core in range(8):
        in_maps.append({
            "x": np.ascontiguousarray(x[2 * core:2 * core + 2]),
            "wqkvT": wqkvT, "qkvb": qkvb, "woT": woT, "ob": ob,
            "wgT": wgT, "wuT": wuT, "wdT": wdT,
        })
    res = run_bass_kernel_spmd(nc, in_maps, core_ids=list(range(8)))
    return np.concatenate([r["out"] for r in res.results], axis=0)


# version 35
# speedup vs baseline: 1.0554x; 1.0554x over previous
import numpy as np
import concourse.bass as bass
import concourse.mybir as mybir
from concourse.bass_utils import run_bass_kernel_spmd

F32 = mybir.dt.float32
BF16 = mybir.dt.bfloat16
I32 = mybir.dt.int32
AF = mybir.ActivationFunctionType
ALU = mybir.AluOpType

P = 128
D = 1024
DC = D // P
H3 = 3072
H3C = H3 // P
HEADS = 16
DH = 64
L = 2048
SEQS = 2
SCALE_K = 0.125
SQ = 1.0 / 127.0
EPS = 1e-5

ENGINES = ("pe", "dve", "act", "pool", "sp")
DMA_SEMS = ("d_st", "d_x0", "d_x1", "d_x2", "d_x3",
            "d_o0", "d_o1",
            "d_wg0", "d_wg1", "d_wu0", "d_wu1", "d_wd0", "d_wd1", "d_wd2", "d_wd3",
            "d_cg0", "d_cg1", "d_cu0", "d_cu1", "d_cd0", "d_cd1")


class Ev:
    __slots__ = ("sem", "eng", "val", "needed")

    def __init__(self, sem, eng):
        self.sem, self.eng, self.val, self.needed = sem, eng, None, False


class Res:
    __slots__ = ("w", "rs")

    def __init__(self):
        self.w, self.rs = None, []


class Plan:
    def __init__(self):
        self.q = {e: [] for e in ENGINES}
        self.last = {e: None for e in ENGINES}
        self.dma_cnt = {s: 0 for s in DMA_SEMS}

    def op(self, eng, emit, reads=(), writes=(), scalar_reads=(), dma=None,
           extra_waits=(), writes_nowait=()):
        waits = []

        def need(ev, force=False):
            if ev is None:
                return
            if ev.eng == eng and not force and eng == "pe":
                return
            ev.needed = True
            waits.append(ev)

        for r in reads:
            need(r.w)
        for r in scalar_reads:
            need(r.w, force=True)
        for w in writes:
            need(w.w)
            for ev in w.rs:
                need(ev)
        for ev in extra_waits:
            need(ev, force=(ev.eng == eng))
        if dma:
            ev = Ev(dma, None)
            self.dma_cnt[dma] += 16
            ev.val = self.dma_cnt[dma]
            ev.needed = True
        else:
            ev = Ev(eng, eng)
        self.q[eng].append((waits, emit, ev, bool(dma)))
        self.last[eng] = ev
        for r in list(reads) + list(scalar_reads):
            r.rs.append(ev)
        for w in list(writes) + list(writes_nowait):
            w.w = ev
            w.rs = []
        return ev

    def assign(self):
        for eng in ENGINES:
            cnt = 0
            for waits, emit, ev, dma in self.q[eng]:
                if not dma and ev.needed:
                    cnt += 1
                    ev.val = cnt

    def emit(self, eng_obj, eng, sems):
        satisfied = {}
        for waits, emit, ev, dma in self.q[eng]:
            best = {}
            for w in waits:
                if w.val is None:
                    continue
                v = self.dma_cnt["d_st"] if w.sem == "d_st" else w.val
                if v > best.get(w.sem, 0):
                    best[w.sem] = v
            for s, v in best.items():
                if satisfied.get(s, 0) < v:
                    eng_obj.wait_ge(sems[s], v)
                    satisfied[s] = v
            instr = emit(eng_obj)
            if ev.needed and instr is not None:
                instr.then_inc(sems[ev.sem], 16 if dma else 1)


def build_core_program(n_halves=4, do_b=True, a_ph=9, a_ch=4):
    nc = bass.Bass("TRN2", target_bir_lowering=False, debug=False)

    x_h = nc.dram_tensor("x", [SEQS, L, D], F32, kind="ExternalInput").ap()
    wqkvT_h = nc.dram_tensor("wqkvT", [D, H3], F32, kind="ExternalInput").ap()
    qkvb_h = nc.dram_tensor("qkvb", [1, H3], F32, kind="ExternalInput").ap()
    woT_h = nc.dram_tensor("woT", [D, D], F32, kind="ExternalInput").ap()
    ob_h = nc.dram_tensor("ob", [1, D], F32, kind="ExternalInput").ap()
    wgT_h = nc.dram_tensor("wgT", [D, H3], I32, kind="ExternalInput").ap()
    wuT_h = nc.dram_tensor("wuT", [D, H3], I32, kind="ExternalInput").ap()
    wdT_h = nc.dram_tensor("wdT", [H3, D], I32, kind="ExternalInput").ap()
    out_h = nc.dram_tensor("out", [SEQS, L, D], F32, kind="ExternalOutput").ap()
    wg_scr = nc.dram_tensor("wg_scr", [H3C, P, DC * P], BF16, kind="Internal").ap()
    wu_scr = nc.dram_tensor("wu_scr", [H3C, P, DC * P], BF16, kind="Internal").ap()
    wd_scr = nc.dram_tensor("wd_scr", [H3C, P, D], BF16, kind="Internal").ap()

    _probe = nc.alloc_sbuf_tensor("_base_probe", [P, 4], F32)
    cur = [nc.lookup_mloc(_probe).addr]

    def at(name, shape, dtype, align=64):
        o = (cur[0] + align - 1) // align * align
        t = nc.alloc_sbuf_tensor_at(name, shape, dtype, offset=o)
        n = 1
        for s in shape[1:]:
            n *= s
        cur[0] = o + n * mybir.dt.size(dtype)
        return t

    wqkv_bf = at("wqkv_bf", [P, DC, H3], BF16)
    wo_bf = at("wo_bf", [P, DC, D], BF16)
    x1_tm = at("x1_tm", [P, 8, D], BF16)
    ident_b = at("ident_b", [P, P], BF16)
    ident_f = at("ident_f", [P, P], F32)
    mask4 = at("mask4", [P, 4, P], BF16)
    qkb_cols = at("qkb_cols", [P, 16], F32)
    ones_bf = at("ones_bf", [1, P], BF16)
    zeros_bf = at("zeros_bf", [1, 512], BF16)
    vb_bcast = at("vb_bcast", [P, D], BF16)
    ob_bf = at("ob_bf", [1, D], BF16)
    S_sb = at("S_sb", [P, 512], BF16)
    wgr = at("wgr", [P, 2, DC, P], BF16)
    wur = at("wur", [P, 2, DC, P], BF16)
    wdr = at("wdr", [P, 4, D], BF16)
    out_sb = at("out_sb", [P, 2, D], F32)
    silu_s = at("silu_s", [P, 2, 512], BF16)
    h1f = at("h1f", [P, 3, D], F32)
    sqs = nc.alloc_sbuf_tensor_at("sqs", [P, D], BF16,
                                  offset=nc.lookup_mloc(silu_s).addr)
    stats = at("stats", [P, 3, 8], F32)
    x1_fm = at("x1_fm", [P, DC, 1024], BF16)
    persist_end = cur[0]

    scratch0 = cur[0]
    x_fm = at("x_fm", [P, DC, 512], BF16)
    qk_fm = at("qk_fm", [P, 16, 512], BF16)
    at_sb = at("at_sb", [P, HEADS, P], BF16)
    y_fm = at("y_fm", [P, DC, P], BF16)
    v_tm = at("v_tm", [P, 4, D], BF16)
    k_tm = at("k_tm", [P, 4, D], BF16)
    qkvb_row = nc.alloc_sbuf_tensor_at("qkvb_row", [1, H3], F32,
                                       offset=scratch0 + 8192)
    vb_row = nc.alloc_sbuf_tensor_at("vb_row", [1, D], BF16,
                                     offset=scratch0 + 8192 + 12288)
    cur[0] = scratch0 + 48 * 1024
    x_tm = at("x_tm", [P, 4, D], F32)
    sweepA_end = cur[0]

    cur[0] = scratch0
    h_fm = at("h_fm", [P, H3C, 1024], BF16)
    sweepB_end = cur[0]

    assert max(sweepA_end, sweepB_end) <= 229376, (persist_end, sweepA_end, sweepB_end)

    pF = nc.alloc_psum_tensor("pF", [P, 7, 512], F32)
    pB = nc.alloc_psum_tensor("pB", [P, 1024], BF16)

    pl = Plan()
    h1rr = [0]

    def h1_next():
        s = h1rr[0] % 3
        h1rr[0] += 1
        return s
    r_wqkv, r_wo = Res(), Res()
    r_consts = Res()
    r_qkbcols = Res()
    r_qrow = Res()
    r_qbf = Res()
    r_obf = Res()
    r_Ssb = Res()
    r_x_tm = Res()
    r_x_tm_s = [Res() for _ in range(4)]
    r_x_fm = Res()
    r_qk = [Res() for _ in range(16)]
    r_v = [Res() for _ in range(4)]
    r_k = [Res() for _ in range(4)]
    r_at = Res()
    r_at_e, r_at_o = Res(), Res()
    r_y = Res()
    r_h1 = [Res(), Res(), Res()]
    r_sqs = Res()
    r_stats = [Res(), Res(), Res()]
    r_x1tm = [Res() for _ in range(8)]
    r_x1fm = [Res(), Res()]
    r_hfm = [Res() for _ in range(H3C)]
    rh_xfm = r_hfm[0:4]
    rh_qk = r_hfm[4:12]
    rh_at = r_hfm[12:14]
    rh_y = r_hfm[14:15]
    rh_v = r_hfm[15:19]
    rh_k = r_hfm[19:23]
    r_silu = [Res(), Res()]
    r_outsb = [Res(), Res()]
    r_wgr = [Res(), Res()]
    r_wur = [Res(), Res()]
    r_wdr = [Res() for _ in range(4)]
    rb = [Res() for _ in range(7)]
    r_pBh = [Res(), Res()]

    overlay = [r_x_tm, *r_x_tm_s, r_x_fm, r_qk, *r_v, *r_k,
               r_at, r_y, *r_hfm]

    def barrier():
        ev = pl.op("pe", lambda e: e.nop(),
                   extra_waits=[x for x in (pl.last["dve"], pl.last["act"],
                                            pl.last["pool"],
                                            last_store[0], last_store[1])
                                if x is not None])
        for r in overlay:
            r.w = ev
            r.rs = []

    def dma(eng, out, in_, **kw):
        return lambda e: e.dma_start(out=out, in_=in_, **kw)

    pl.op("pool", dma("pool", qkvb_row[:], qkvb_h), writes_nowait=[r_qrow], dma="d_st")
    pl.op("pool", dma("pool", ob_bf[:], ob_h), writes_nowait=[r_obf], dma="d_st")

    pl.op("pool", lambda e: e.memset(ident_b[:], 0.0), writes=[r_consts])
    pl.op("pool", lambda e: e.affine_select(
        out=ident_b[:], in_=ident_b[:], compare_op=ALU.not_equal,
        fill=1.0, base=0, pattern=[[-1, P]], channel_multiplier=1),
        writes=[r_consts])
    pl.op("pool", lambda e: e.memset(ident_f[:], 0.0), writes=[r_consts])
    pl.op("pool", lambda e: e.affine_select(
        out=ident_f[:], in_=ident_f[:], compare_op=ALU.not_equal,
        fill=1.0, base=0, pattern=[[-1, P]], channel_multiplier=1),
        writes=[r_consts])
    pl.op("pool", lambda e: e.memset(mask4[:], 1.0), writes=[r_consts])
    pl.op("pool", lambda e: e.affine_select(
        out=mask4[:], in_=mask4[:], compare_op=ALU.is_ge,
        fill=0.0, base=0, pattern=[[0, 4], [1, P]], channel_multiplier=-1),
        writes=[r_consts])
    pl.op("pool", lambda e: e.memset(ones_bf[:], 1.0), writes=[r_consts])
    pl.op("pool", lambda e: e.memset(zeros_bf[:], 0.0), writes=[r_consts])

    for kc in range(DC):
        pl.op("pool", dma("pool", wqkv_bf[:, kc, :],
                          wqkvT_h[kc * P:(kc + 1) * P, :],
                          max_dma_last_dim=8192),
              writes_nowait=[r_wqkv], dma="d_st")
    for kc in range(DC):
        pl.op("pool", dma("pool", wo_bf[:, kc, :],
                          woT_h[kc * P:(kc + 1) * P, :],
                          max_dma_last_dim=8192),
              writes_nowait=[r_wo], dma="d_st")

    def qkb_tr(e):
        last = None
        for oc in range(16):
            last = e.transpose(pF[:, 1, oc:oc + 1],
                               qkvb_row[0:1, oc * P:(oc + 1) * P],
                               ident_f[0:1, 0:1])
        return last

    pl.op("pe", qkb_tr, reads=[r_qrow, r_consts], writes=[rb[1]])
    pl.op("dve", lambda e: e.tensor_copy(qkb_cols[:], pF[:, 1, 0:16]),
          reads=[rb[1]], writes=[r_qkbcols])
    pl.op("dve", lambda e: e.tensor_scalar_mul(qkb_cols[:, 8:16],
                                               qkb_cols[:, 8:16], SCALE_K),
          writes=[r_qkbcols], scalar_reads=[r_qkbcols])
    pl.op("pool", lambda e: e.tensor_copy(vb_row[:], qkvb_row[0:1, 2048:3072]),
          reads=[r_qrow], writes=[r_qbf])
    def vb_mm(e):
        e.matmul(pF[:, 5, :], ones_bf[0:1, :], vb_row[0:1, 0:512],
                 start=True, stop=True)
        return e.matmul(pF[:, 6, :], ones_bf[0:1, :], vb_row[0:1, 512:1024],
                        start=True, stop=True)
    pl.op("pe", vb_mm, reads=[r_qbf, r_consts], writes=[rb[5], rb[6]])
    pl.op("dve", lambda e: e.tensor_copy(
        vb_bcast[:], pF[:, 5:7, :].rearrange("p a n -> p (a n)")),
        reads=[rb[5], rb[6]], writes=[r_qbf])

    wg_src = wgT_h.rearrange("(kc p) h -> p kc h", p=P)
    wu_src = wuT_h.rearrange("(kc p) h -> p kc h", p=P)
    wd_src = wdT_h.rearrange("(hc p) d -> p hc d", p=P)

    r_scr_g = [Res() for _ in range(H3C)]
    r_scr_u = [Res() for _ in range(H3C)]
    r_scr_d = [Res() for _ in range(H3C)]

    r_cvg = [Res(), Res()]
    r_cvu = [Res(), Res()]
    r_cvd = [Res(), Res()]

    def wconv_thunks():
        thunks = []

        def gu_pair(i):
            s = slice(i * P, (i + 1) * P)
            pl.op("pool", dma("pool",
                              wg_scr[i].rearrange("p (a b) -> p a b", a=DC),
                              wg_src[:, :, s]),
                  writes=[r_cvg[i % 2]], writes_nowait=[r_scr_g[i]],
                  dma="d_cg%d" % (i % 2))
            pl.op("pool", dma("pool",
                              wu_scr[i].rearrange("p (a b) -> p a b", a=DC),
                              wu_src[:, :, s]),
                  writes=[r_cvu[i % 2]], writes_nowait=[r_scr_u[i]],
                  dma="d_cu%d" % (i % 2))

        def wd_piece(hc):
            pl.op("pool", dma("pool", wd_scr[hc],
                              wdT_h[hc * P:(hc + 1) * P, :]),
                  writes=[r_cvd[hc % 2]], writes_nowait=[r_scr_d[hc]],
                  dma="d_cd%d" % (hc % 2))

        for i in range(H3C):
            thunks.append(lambda i=i: gu_pair(i))
        for hc in range(H3C):
            thunks.append(lambda hc=hc: wd_piece(hc))
        return thunks

    def load_gu(i, slot):
        pl.op("sp", dma("sp", wgr[:, slot, :, :],
                        wg_scr[i].rearrange("p (a b) -> p a b", a=DC)),
              reads=[r_scr_g[i]], writes=[r_wgr[slot]], dma="d_wg%d" % slot)
        pl.op("sp", dma("sp", wur[:, slot, :, :],
                        wu_scr[i].rearrange("p (a b) -> p a b", a=DC)),
              reads=[r_scr_u[i]], writes=[r_wur[slot]], dma="d_wu%d" % slot)

    def load_wd(hc, slot):
        pl.op("sp", dma("sp", wdr[:, slot, :], wd_scr[hc]),
              reads=[r_scr_d[hc]], writes=[r_wdr[slot]], dma="d_wd%d" % slot)

    def a_head_ops(seq, half, t, with_loader=False):
        tok0 = half * 1024 + t * 512
        gt = (seq * 2 + half) * 2 + t
        thunks = []

        def load_piece(s):
            sl = (gt + s) % 4
            pl.op("sp", dma("sp", x_tm[:, sl, :],
                            x_h[seq, tok0 + s * P:tok0 + (s + 1) * P, :]),
                  writes=[r_x_tm_s[sl]], dma="d_x%d" % sl)

        def loads():
            for s in range(4):
                load_piece(s)
        thunks.append(('x', loads))
        thunks.append(('piece_loader', load_piece))

        def xtrs(s0):
            for s in (s0, s0 + 1):
                sl = (gt + s) % 4

                def xtr(e, sl=sl):
                    last = None
                    for fc in range(DC):
                        last = e.transpose(
                            pF[:, 1 + fc // 4, (fc % 4) * P:(fc % 4 + 1) * P],
                            x_tm[:, sl, fc * P:(fc + 1) * P], ident_f[:])
                    return last

                pl.op("pe", xtr, reads=[r_x_tm_s[sl], r_consts],
                      writes=[rb[1], rb[2]])

                def xtr_ev(e, s=s):
                    e.tensor_copy(x_fm[:, 0:4, s * P:(s + 1) * P], pF[:, 1, :])
                    return e.tensor_copy(x_fm[:, 4:8, s * P:(s + 1) * P],
                                         pF[:, 2, :])

                pl.op("dve", xtr_ev, reads=[rb[1], rb[2]],
                      writes=[r_x_fm] + rh_xfm)
        thunks.append(('q', lambda: xtrs(0)))
        thunks.append(('q', lambda: xtrs(2)))

        def qkvs(oc0):
            for oc in range(oc0, oc0 + 4):
                b = 1 + (oc % 2)

                def qk_mm(e, oc=oc, b=b):
                    last = None
                    for kc in range(DC):
                        last = e.matmul(pF[:, b, :],
                                        wqkv_bf[:, kc, oc * P:(oc + 1) * P],
                                        x_fm[:, kc, :],
                                        start=(kc == 0), stop=(kc == DC - 1))
                    return last

                pl.op("pe", qk_mm, reads=[r_x_fm, r_wqkv], writes=[rb[b]])
                if oc >= 8:
                    def qk_ev(e, oc=oc, b=b):
                        return e.activation(qk_fm[:, oc, :], pF[:, b, :],
                                            AF.Identity,
                                            bias=qkb_cols[:, oc:oc + 1],
                                            scale=SCALE_K)

                    pl.op("act", qk_ev, reads=[rb[b]],
                          writes=[r_qk[oc]] + rh_qk,
                          scalar_reads=[r_qkbcols])
                else:
                    def qk_ev(e, oc=oc, b=b):
                        return e.tensor_scalar(qk_fm[:, oc, :], pF[:, b, :],
                                               1.0, qkb_cols[:, oc:oc + 1],
                                               ALU.mult, ALU.add)

                    pl.op("dve", qk_ev, reads=[rb[b]],
                          writes=[r_qk[oc]] + rh_qk,
                          scalar_reads=[r_qkbcols])
        for oc0 in (8, 12, 0, 4):
            thunks.append(('q', lambda oc0=oc0: qkvs(oc0)))
        loader = thunks.pop(1)[1]
        if with_loader:
            return thunks, loader
        return thunks

    def emit_x1tr(gc):
        def x1tr(e, gc=gc):
            last = None
            for fc in range(DC):
                last = e.transpose(pB[:, fc * P:(fc + 1) * P],
                                   x1_tm[:, gc, fc * P:(fc + 1) * P],
                                   ident_b[:])
            return last

        pl.op("pe", x1tr, reads=[r_x1tm[gc], r_consts], writes=r_pBh)
        pl.op("dve", lambda e, gc=gc: e.tensor_copy(
            x1_fm[:, :, gc * P:(gc + 1) * P],
            pB[:].rearrange("p (a b) -> p a b", a=DC)),
            reads=r_pBh, writes=[r_x1fm[gc // 4]])

    def a_tile_rest(seq, half, t, x1tr_pending, early=()):
        early = list(early)
        """V+ktr phase and the software-pipelined chunk loop for tile t."""
        gt = (seq * 2 + half) * 2 + t
        for s in range(4):
            def v_mm(e, s=s):
                last = None
                for n2 in range(2):
                    b = 3 + n2
                    cols = slice(2048 + n2 * 512, 2048 + (n2 + 1) * 512)
                    for kc in range(DC):
                        last = e.matmul(pF[:, b, :],
                                        x_fm[:, kc, s * P:(s + 1) * P],
                                        wqkv_bf[:, kc, cols],
                                        start=(kc == 0), stop=(kc == DC - 1))
                return last

            pl.op("pe", v_mm, reads=[r_x_fm, r_wqkv, r_qbf, r_consts],
                  writes=[rb[3], rb[4]])
            pl.op("act", lambda e, s=s: e.activation(
                v_tm[:, s, :],
                pF[:, 3:5, :].rearrange("p a n -> p (a n)"), AF.Copy),
                reads=[rb[3], rb[4]], writes=[r_v[s]] + rh_v)
            pl.op("pool", lambda e, s=s: e.tensor_add(
                v_tm[:, s, :], v_tm[:, s, :], vb_bcast[:]),
                reads=[r_qbf], writes=[r_v[s]])

            def ktr(e, c=s):
                last = None
                for fc in range(DC):
                    last = e.transpose(pB[:, fc * P:(fc + 1) * P],
                                       qk_fm[:, 8 + fc, c * P:(c + 1) * P],
                                       ident_b[:])
                return last

            pl.op("pe", ktr, reads=r_qk[8:16] + [r_consts], writes=r_pBh)
            pl.op("act", lambda e, c=s: e.activation(k_tm[:, c, :], pB[:],
                                                     AF.Copy),
                  reads=r_pBh, writes=[r_k[s]] + rh_k)

        def at_slot(h):
            return (h % 2) * 8 + (h // 8) * 4 + (h % 8) // 2

        def at_par_f(c, par):
            def at_par(e):
                last = None
                if par == 1:
                    e.drain()
                for h in range(par, HEADS, 2):
                    sl = at_slot(h)
                    bank = 1 + sl // 4
                    q = sl % 4
                    qc = h // 2
                    r0 = 64 * par
                    last = e.matmul(
                        pF[:, bank, q * P:(q + 1) * P],
                        qk_fm[r0:r0 + 64, 8 + qc, c * P:(c + 1) * P],
                        qk_fm[r0:r0 + 64, qc, c * P:(c + 1) * P],
                        start=(q == 0), stop=(q == 3),
                        skip_group_check=True)
                return last
            return at_par

        def y_zero_f():
            def y_zero(e):
                e.matmul(pF[:, 5, :], ones_bf[0:1, :], zeros_bf[0:1, :],
                         start=True, stop=False, skip_group_check=True)
                return e.matmul(pF[:, 6, :], ones_bf[0:1, :], zeros_bf[0:1, :],
                                start=True, stop=False, skip_group_check=True)
            return y_zero

        def y_intra_f(c, g, par):
            def y_intra(e):
                last = None
                for h in range(par, HEADS, 2):
                    pair = h // 2
                    r0 = 64 * (h % 2)
                    last = e.matmul(
                        pF[r0:r0 + 64, 5 + pair // 4,
                           (pair % 4) * P:(pair % 4 + 1) * P],
                        v_tm[:, c, h * 64:(h + 1) * 64],
                        at_sb[:, at_slot(h), :],
                        start=(pair % 4 == 0), stop=(g == 0),
                        skip_group_check=True)
                return last
            return y_intra

        def y_cross_f(c):
            def y_cross(e):
                last = None
                for h in range(HEADS):
                    pair = h // 2
                    r0 = 64 * (h % 2)
                    last = e.matmul(
                        pF[r0:r0 + 64, 5 + pair // 4,
                           (pair % 4) * P:(pair % 4 + 1) * P],
                        S_sb[r0:r0 + 64, pair * 64:(pair + 1) * 64],
                        qk_fm[r0:r0 + 64, pair, c * P:(c + 1) * P],
                        start=False, stop=True,
                        skip_group_check=True)
                return last
            return y_cross

        def s_upd_f(c, g):
            def s_upd(e):
                last = None
                for h in range(HEADS):
                    r0 = 64 * (h % 2)
                    pair = h // 2
                    last = e.matmul(
                        pF[r0:r0 + 64, 0, pair * 64:(pair + 1) * 64],
                        k_tm[:, c, h * 64:(h + 1) * 64],
                        v_tm[:, c, h * 64:(h + 1) * 64],
                        start=False, stop=(g == 14),
                        skip_group_check=True)
                return last
            return s_upd

        def op_bias_f():
            def op_bias(e):
                e.matmul(pF[:, 5, :], ones_bf[0:1, :], ob_bf[0:1, 0:512],
                         start=True, stop=False, skip_group_check=True)
                return e.matmul(pF[:, 6, :], ones_bf[0:1, :],
                                ob_bf[0:1, 512:1024],
                                start=True, stop=False, skip_group_check=True)
            return op_bias

        def op_mm_f():
            def op_mm(e):
                last = None
                for n2 in range(2):
                    b = 5 + n2
                    cols = slice(n2 * 512, (n2 + 1) * 512)
                    for kc in range(DC):
                        last = e.matmul(pF[:, b, :], y_fm[:, kc, :],
                                        wo_bf[:, kc, cols],
                                        start=False, stop=(kc == DC - 1),
                                        skip_group_check=True)
                return last
            return op_mm

        for it in range(5):
            c = it
            g = (half * 2 + t) * 4 + c
            if c <= 3:
                if g > 0:
                    pl.op("act", lambda e: e.activation(
                        S_sb[:], pF[:, 0, :], AF.Copy),
                        reads=[rb[0]], writes=[r_Ssb])
                pl.op("pe", at_par_f(c, 0), reads=list(r_qk),
                      writes=[rb[1], rb[2]])
                pl.op("dve", lambda e: e.tensor_tensor(
                    at_sb[:, 0:4, :], pF[:, 1, :], mask4[:], ALU.mult),
                    reads=[rb[1], r_consts], writes=[r_at_e] + rh_at)
                pl.op("act", lambda e: e.activation(
                    at_sb[:, 4:8, :], pF[:, 2, :], AF.Copy),
                    reads=[rb[2]], writes=[r_at_e] + rh_at)
                pl.op("pool", lambda e: e.tensor_tensor(
                    at_sb[:, 4:8, :], at_sb[:, 4:8, :], mask4[:], ALU.mult),
                    reads=[r_consts], writes=[r_at_e] + rh_at)
                pl.op("pe", at_par_f(c, 1), reads=list(r_qk),
                      writes=[rb[3], rb[4]])
                pl.op("dve", lambda e: e.tensor_tensor(
                    at_sb[:, 8:12, :], pF[:, 3, :], mask4[:], ALU.mult),
                    reads=[rb[3], r_consts], writes=[r_at_o] + rh_at)
                pl.op("act", lambda e: e.activation(
                    at_sb[:, 12:16, :], pF[:, 4, :], AF.Copy),
                    reads=[rb[4]], writes=[r_at_o] + rh_at)
                pl.op("pool", lambda e: e.tensor_tensor(
                    at_sb[:, 12:16, :], at_sb[:, 12:16, :], mask4[:],
                    ALU.mult),
                    reads=[r_consts], writes=[r_at_o] + rh_at)
            if it >= 1:
                cp = it - 1
                pl.op("pe", op_bias_f(), reads=[r_obf, r_consts],
                      writes=[rb[5], rb[6]])
                pl.op("pe", op_mm_f(), reads=[r_y, r_wo],
                      writes=[rb[5], rb[6]])
                slp = (gt + cp) % 4
                par = h1_next()
                pl.op("dve", lambda e, slp=slp, par=par: e.scalar_tensor_tensor(
                    h1f[:, par, :],
                    pF[:, 5:7, :].rearrange("p a n -> p (a n)"),
                    1.0, x_tm[:, slp, :], ALU.mult, ALU.add,
                    accum_out=stats[:, par, 0:1]),
                    reads=[rb[5], rb[6], r_x_tm_s[slp]],
                    writes=[r_h1[par], r_stats[par]])
                lim = 0 if it == 4 else 2
                while len(x1tr_pending) > lim:
                    emit_x1tr(x1tr_pending.pop(0))
                ln_norm(x1_tm[:, t * 4 + cp, :], [r_x1tm[t * 4 + cp]],
                        par)
                for _ in range(2):
                    if wconv:
                        wconv.pop(0)()
                x1tr_pending.append(t * 4 + cp)
                if early and 2 <= it <= 4:
                    early[0](it - 2)
            if c <= 3:
                if g < 15:
                    pl.op("pe", s_upd_f(c, g), reads=[r_k[c], r_v[c]],
                          writes=[rb[0]])
                pl.op("pe", y_intra_f(c, g, 0),
                      reads=[r_v[c], r_at_e, r_consts],
                      writes=[rb[5], rb[6]])
                pl.op("pe", y_intra_f(c, g, 1),
                      reads=[r_v[c], r_at_o, r_consts],
                      writes=[rb[5], rb[6]])
                if g > 0:
                    pl.op("pe", y_cross_f(c),
                          reads=[r_Ssb, r_consts] + r_qk[0:8],
                          writes=[rb[5], rb[6]])

                def y_ev(e):
                    e.activation(y_fm[:, 0:4, :], pF[:, 5, :], AF.Copy)
                    return e.activation(y_fm[:, 4:8, :], pF[:, 6, :],
                                        AF.Copy)

                pl.op("act", y_ev, reads=[rb[5], rb[6]],
                      writes=[r_y] + rh_y)
        if early:
            early[0](3)
        if t == 1:
            while len(x1tr_pending) > 1:
                emit_x1tr(x1tr_pending.pop(0))

    def ln_norm(dest_ap, dest_res_list, par):
        src_ap = h1f[:, par, :]
        sres = r_h1[par]
        rst = r_stats[par]
        st = stats[:, par, :]
        pl.op("act", lambda e: e.activation(sqs[:], src_ap, AF.Square,
                                            accum_out=st[:, 1:2]),
              reads=[sres], writes=[r_sqs, rst, r_silu[0], r_silu[1]])
        pl.op("dve", lambda e: e.tensor_scalar_mul(
            st[:, 2:3], st[:, 0:1], -1.0 / D),
            reads=[rst], writes=[rst])
        pl.op("dve", lambda e: e.tensor_scalar(
            st[:, 3:4], st[:, 2:3], st[:, 2:3], -EPS,
            ALU.mult, ALU.add),
            reads=[rst], writes=[rst], scalar_reads=[rst])
        pl.op("dve", lambda e: e.scalar_tensor_tensor(
            st[:, 4:5], st[:, 1:2], 1.0 / D,
            st[:, 3:4], ALU.mult, ALU.subtract),
            reads=[rst], writes=[rst])
        pl.op("act", lambda e: e.activation(st[:, 5:6], st[:, 4:5],
                                            AF.Sqrt),
              reads=[rst], writes=[rst])
        pl.op("dve", lambda e: e.reciprocal(st[:, 6:7], st[:, 5:6]),
              reads=[rst], writes=[rst])
        pl.op("pool", lambda e: e.tensor_scalar(dest_ap, src_ap,
                                                st[:, 2:3], st[:, 6:7],
                                                ALU.add, ALU.mult),
              reads=[sres], writes=dest_res_list, scalar_reads=[rst])

    def sweep_b(seq, half, a_tail, next_head, first=False, x1p=None):
        if first:
            load_gu(0, 0)
            load_gu(1, 1)
        heads = list(next_head) if next_head else []
        if heads:
            heads.pop(0)[1]()
        if not first:
            load_wd(0, 0)
        for i in range(H3C):
            slot = i % 2
            for t2 in range(2):
                bg, bu = 1 + 2 * t2, 2 + 2 * t2

                def g_mm(e, i=i, t2=t2, slot=slot, b=bg):
                    last = None
                    for kc in range(DC):
                        last = e.matmul(pF[:, b, :], wgr[:, slot, kc, :],
                                        x1_fm[:, kc, t2 * 512:(t2 + 1) * 512],
                                        start=(kc == 0), stop=(kc == DC - 1))
                    return last

                def u_mm(e, i=i, t2=t2, slot=slot, b=bu):
                    last = None
                    for kc in range(DC):
                        last = e.matmul(pF[:, b, :], wur[:, slot, kc, :],
                                        x1_fm[:, kc, t2 * 512:(t2 + 1) * 512],
                                        start=(kc == 0), stop=(kc == DC - 1))
                    return last

                pl.op("pe", g_mm, reads=[r_x1fm[t2], r_wgr[slot]],
                      writes=[rb[bg]])
                pl.op("pe", u_mm, reads=[r_x1fm[t2], r_wur[slot]],
                      writes=[rb[bu]])
                if x1p and i == 0 and t2 == 0:
                    emit_x1tr(x1p.pop(0))
                pl.op("act", lambda e, t2=t2, b=bg: e.activation(
                    silu_s[:, t2, :], pF[:, b, :], AF.Silu, scale=SQ),
                    reads=[rb[bg]], writes=[r_silu[t2]])
                pl.op("dve", lambda e, i=i, t2=t2, b=bu: e.tensor_tensor(
                    h_fm[:, i, t2 * 512:(t2 + 1) * 512], silu_s[:, t2, :],
                    pF[:, b, :], ALU.mult),
                    reads=[r_silu[t2], rb[bu]], writes=[r_hfm[i]],
                    extra_waits=(a_tail if (i == 0 and t2 == 0) else ()))
            if i + 2 < H3C:
                load_gu(i + 2, slot)
            if not first:
                if i < 3:
                    load_wd(i + 1, i + 1)
            elif 8 <= i < 12:
                load_wd(i - 8, i - 8)
            for _ in range(2):
                if wconv:
                    wconv.pop(0)()

        groups = [(0, 1, 2), (3, 4, 5), (6, 7)]
        hc_orders = [list(range(H3C)), list(range(H3C - 1, -1, -1)),
                     list(range(H3C))]
        wd_order = [(g2, hc) for g2 in range(3) for hc in hc_orders[g2]]
        widx = [4]

        def wd_prefetch():
            if widx[0] < len(wd_order):
                g2, hc2 = wd_order[widx[0]]
                widx[0] += 1
                resident = (g2 == 1 and hc2 >= H3C - 4) or \
                           (g2 == 2 and hc2 < 4)
                if not resident:
                    load_wd(hc2, hc2 % 4)

        fin_q = []

        def finalize(gc, par):
            def th():
                slot = gc % 2
                ln_norm(out_sb[:, slot, :], [r_outsb[slot]], par)
                tok0 = half * 1024 + gc * P
                ev_st = pl.op("sp", dma("sp", out_h[seq, tok0:tok0 + P, :],
                                        out_sb[:, slot, :]),
                              reads=[r_outsb[slot]], dma="d_o%d" % slot)
                last_store[slot] = ev_st
            return th

        for grp, gcs in enumerate(groups):
            for idx, hc in enumerate(hc_orders[grp]):
                slot = hc % 4
                if idx == 0:
                    for j, gc in enumerate(gcs):
                        def dn0(e, j=j, gc=gc, hc=hc, slot=slot):
                            last = None
                            for n2 in range(2):
                                last = e.matmul(
                                    pF[:, 1 + 2 * j + n2, :],
                                    h_fm[:, hc, gc * P:(gc + 1) * P],
                                    wdr[:, slot, n2 * 512:(n2 + 1) * 512],
                                    start=True, stop=False,
                                    skip_group_check=True)
                            return last
                        pl.op("pe", dn0, reads=[r_hfm[hc], r_wdr[slot]],
                              writes=[rb[1 + 2 * j], rb[2 + 2 * j]])
                else:
                    def dn_mm(e, gcs=gcs, hc=hc, slot=slot, idx=idx):
                        last = None
                        for j, gc in enumerate(gcs):
                            for n2 in range(2):
                                last = e.matmul(
                                    pF[:, 1 + 2 * j + n2, :],
                                    h_fm[:, hc, gc * P:(gc + 1) * P],
                                    wdr[:, slot, n2 * 512:(n2 + 1) * 512],
                                    start=False, stop=(idx == H3C - 1),
                                    skip_group_check=True)
                        return last

                    bank_res = [rb[1 + 2 * j + n2] for j in range(len(gcs))
                                for n2 in range(2)]
                    pl.op("pe", dn_mm, reads=[r_hfm[hc], r_wdr[slot]],
                          writes=bank_res)
                wd_prefetch()
                if fin_q:
                    fin_q.pop(0)()
            for j, gc in enumerate(gcs):
                par = h1_next()

                def h2_ev(e, j=j, gc=gc, par=par):
                    return e.scalar_tensor_tensor(
                        h1f[:, par, :],
                        pF[:, 1 + 2 * j:3 + 2 * j, :]
                        .rearrange("p a n -> p (a n)"),
                        SQ * SQ, x1_tm[:, gc, :], ALU.mult, ALU.add,
                        accum_out=stats[:, par, 0:1])

                pl.op("dve", h2_ev,
                      reads=[rb[1 + 2 * j], rb[2 + 2 * j], r_x1tm[gc]],
                      writes=[r_h1[par], r_stats[par]])
                fin_q.append(finalize(gc, par))
                while heads and (heads[0][0] == 'x' or grp == 2):
                    heads.pop(0)[1]()
                    break
        while fin_q:
            fin_q.pop(0)()
        for _, th in heads:
            th()

    last_store = [None, None]
    head_done = [False]
    for _, th in a_head_ops(0, 0, 0):
        th()
    head_done[0] = True
    wconv = wconv_thunks()
    n_done = 0
    for seq in range(SEQS):
        if n_done >= n_halves:
            break
        pl.op("pe", lambda e: e.matmul(pF[:, 0, :], ones_bf[0:1, :],
                                       zeros_bf[0:1, :], start=True, stop=True,
                                       skip_group_check=True),
              reads=[r_consts], writes=[rb[0]])
        for half in range(2):
            if n_done >= n_halves:
                break
            if not head_done[0]:
                for _, th in a_head_ops(seq, half, 0):
                    th()
            x1p = []
            heads1, loader1 = a_head_ops(seq, half, 1, with_loader=True)
            a_tile_rest(seq, half, 0, x1p, early=[loader1])
            for tag, th in heads1:
                if tag != 'x':
                    th()
            a_tile_rest(seq, half, 1, x1p)
            head_done[0] = False
            if do_b:
                a_tail = [x for x in (pl.last["pe"],) if x is not None]
                nseq, nhalf = (seq, 1) if half == 0 else (seq + 1, 0)
                nxt = None
                if n_done + 1 < n_halves and nseq < SEQS and do_b:
                    nxt = a_head_ops(nseq, nhalf, 0)
                    head_done[0] = True
                sweep_b(seq, half, a_tail, nxt, first=(n_done == 0),
                        x1p=x1p)
            n_done += 1
            if not (seq == SEQS - 1 and half == 1):
                load_gu(0, 0)
                load_gu(1, 1)

    fw = [e for e in last_store if e is not None]
    for e in fw:
        e.needed = True
    if fw:
        pl.q["sp"].append((fw, lambda e: None, Ev("sp", "sp"), False))

    pl.assign()

    sems = {}
    import contextlib
    ctx = contextlib.ExitStack()
    for name in ENGINES + DMA_SEMS:
        sems[name] = ctx.enter_context(nc.semaphore(name))

    with nc.Block() as blk:
        @blk.tensor
        def _(e):
            pl.emit(e, "pe", sems)

        @blk.vector
        def _(e):
            pl.emit(e, "dve", sems)

        @blk.scalar
        def _(e):
            pl.emit(e, "act", sems)

        @blk.gpsimd
        def _(e):
            pl.emit(e, "pool", sems)

        @blk.sync
        def _(e):
            pl.emit(e, "sp", sems)

    ctx.close()
    return nc


_NC_CACHE = {}


def kernel(L0_attn_x, L0_attn_Wqkv_w, L0_attn_Wqkv_b, L0_attn_out_w,
           L0_attn_out_b, L0_ffn_w_gate, L0_ffn_w_up, L0_ffn_w_down):
    x = np.ascontiguousarray(np.asarray(L0_attn_x, dtype=np.float32))
    wqkvT = np.ascontiguousarray(np.asarray(L0_attn_Wqkv_w, np.float32).T)
    qkvb = np.ascontiguousarray(np.asarray(L0_attn_Wqkv_b, np.float32)
                                .reshape(1, H3))
    woT = np.ascontiguousarray(np.asarray(L0_attn_out_w, np.float32).T)
    ob = np.ascontiguousarray(np.asarray(L0_attn_out_b, np.float32)
                              .reshape(1, D))
    wgT = np.ascontiguousarray(np.asarray(L0_ffn_w_gate, np.int32).T)
    wuT = np.ascontiguousarray(np.asarray(L0_ffn_w_up, np.int32).T)
    wdT = np.ascontiguousarray(np.asarray(L0_ffn_w_down, np.int32).T)

    if "nc" not in _NC_CACHE:
        _NC_CACHE["nc"] = build_core_program()
    nc = _NC_CACHE["nc"]

    in_maps = []
    for core in range(8):
        in_maps.append({
            "x": np.ascontiguousarray(x[2 * core:2 * core + 2]),
            "wqkvT": wqkvT, "qkvb": qkvb, "woT": woT, "ob": ob,
            "wgT": wgT, "wuT": wuT, "wdT": wdT,
        })
    res = run_bass_kernel_spmd(nc, in_maps, core_ids=list(range(8)))
    return np.concatenate([r["out"] for r in res.results], axis=0)
